# version 02
# baseline (speedup 1.0000x reference)
import contextlib
import math
import numpy as np
import ml_dtypes
import concourse.bass as bass
import concourse.mybir as mybir
from concourse.bass_utils import run_bass_kernel_spmd

F32 = mybir.dt.float32
BF16 = mybir.dt.bfloat16
U8 = mybir.dt.uint8
U32 = mybir.dt.uint32
AF = mybir.ActivationFunctionType
ALU = mybir.AluOpType
AX = mybir.AxisListType

D = 2048
NCD = 16
T = 2048
B = 4
WIN = 1536
NWC = 12
NTQ = 1024
NQT = 8
NH = 16
NKV = 4
HD = 128
NGL = 32
SCALE = 1.0 / math.sqrt(128.0)
NE = 32
NJ = 12
NDJ = 6
NM = 16
EG = 2
NGRP = NE // EG
NEG = -30000.0 * 8.0

WK = [128 * (min(k, 7) - max(k - 4, 0) + 1) for k in range(NWC)]
QB = [max(k - 4, 0) * 128 for k in range(NWC)]
MCOFF = [0]
for k in range(NWC):
    MCOFF.append(MCOFF[-1] + WK[k])
MCTOT = MCOFF[-1]

ENGS = ["sp", "act", "dve", "pe", "pool"]
ENG_ATTR = {"sp": "sync", "act": "scalar", "dve": "vector", "pe": "tensor", "pool": "gpsimd"}


class Buf:
    __slots__ = ("name", "w", "rs", "excl")

    def __init__(self, name, excl=False):
        self.name = name
        self.w = None
        self.rs = []
        self.excl = excl


class MK:

    def __init__(self, nc):
        self.nc = nc
        self.prog = {e: [] for e in ENGS}
        self.cnt = {}
        self.lastw = {e: {} for e in ENGS}
        self.sem_names = set()
        self.nbuf = 0
        self.pending_reads = {e: [] for e in ENGS}

    def buf(self, name=None, excl=False):
        self.nbuf += 1
        return Buf(name or f"b{self.nbuf}", excl)

    def _bump(self, sem, n):
        self.sem_names.add(sem)
        self.cnt[sem] = self.cnt.get(sem, 0) + n
        return self.cnt[sem]

    def op(self, eng, emit, reads=(), writes=(), inc=True, dma=False):
        need = {}
        for b in reads:
            if b.w:
                s, v = b.w
                if v > need.get(s, 0):
                    need[s] = v
            if b.excl:
                for (s, v) in b.rs:
                    if s != eng and not s.startswith("qd_") and v > need.get(s, 0):
                        need[s] = v
        for b in writes:
            if b.w:
                s, v = b.w
                if v > need.get(s, 0):
                    need[s] = v
            for (s, v) in b.rs:
                if v > need.get(s, 0):
                    need[s] = v
        lw = self.lastw[eng]
        waits = []
        for s, v in need.items():
            if v > lw.get(s, 0):
                waits.append((s, v))
                lw[s] = v
        if inc:
            if dma:
                assert len(writes) == 1, "dma ops must declare exactly one written buf"
                sem = "qd_" + writes[0].name
                n = 16
            else:
                sem = eng
                n = 1
            val = self._bump(sem, n)
            mytok = (sem, val)
            self.prog[eng].append((waits, emit, (sem, n)))
        else:
            mytok = None
            self.prog[eng].append((waits, emit, None))
        if mytok is not None:
            for b in reads:
                b.rs.append(mytok)
            for b in self.pending_reads[eng]:
                b.rs.append(mytok)
            self.pending_reads[eng] = []
            for b in writes:
                b.w = mytok
                b.rs = []
        else:
            for b in reads:
                self.pending_reads[eng].append(b)
        return mytok

    def wait_all(self, eng, bufs):
        need = {}
        for b in bufs:
            if b.w:
                s, v = b.w
                if v > need.get(s, 0):
                    need[s] = v
        lw = self.lastw[eng]
        for s, v in sorted(need.items()):
            if v > lw.get(s, 0):
                self.prog[eng].append(([(s, v)], None, None))
                lw[s] = v

    def finish(self, stack):
        sems = {}
        for s in sorted(self.sem_names):
            sems[s] = stack.enter_context(self.nc.semaphore(s))
        block = stack.enter_context(self.nc.Block())
        for e in ENGS:
            entries = self.prog[e]
            if not entries:
                continue

            def body(E, entries=entries):
                for waits, emit, inc in entries:
                    for (s, v) in waits:
                        E.wait_ge(sems[s], v)
                    if emit is not None:
                        inst = emit(E)
                        if inc is not None:
                            sem, n = inc
                            inst.then_inc(sems[sem], n)
            getattr(block, ENG_ATTR[e])(body)
        return sems


class Arena:

    def __init__(self, mk, tensor, nbytes):
        self.mk = mk
        self.t = tensor
        self.n = nbytes
        self.regions = {}
        self.ghosts = []
        self.peak = 0

    def _overlaps(self, off, size):
        out = []
        for nm, (o, s, b, _, _) in self.regions.items():
            if off < o + s and o < off + size:
                out.append((nm, b))
        return out

    def alloc(self, name, cols, dt, nbufs=1, side="lo"):
        esz = {F32: 4, BF16: 2, U32: 4}[dt]
        size = ((cols * esz + 63) // 64) * 64
        views, bufs = [], []
        for i in range(nbufs):
            off = self._fit(size) if side == "lo" else self._fit_hi(size)
            nm = name if nbufs == 1 else f"{name}{i}"
            nb = self.mk.buf(nm)
            for (go, gs, gb) in self.ghosts:
                if off < go + gs and go < off + size:
                    if gb.w:
                        nb.rs.append(gb.w)
                    nb.rs.extend(gb.rs)
            for onm, ob in self._overlaps(off, size):
                if ob.w:
                    nb.rs.append(ob.w)
                nb.rs.extend(ob.rs)
            self.regions[nm] = (off, size, nb, dt, cols)
            self.peak = max(self.peak, off + size)
            v = self.t[:, off:off + cols * esz].bitcast(dt)
            views.append(v)
            bufs.append(nb)
        if nbufs == 1:
            return views[0], bufs[0]
        return views, bufs

    def _fit(self, size):
        taken = sorted((o, o + s) for (o, s, _, _, _) in self.regions.values())
        cur = 0
        for a, bnd in taken:
            if a - cur >= size:
                return cur
            cur = max(cur, bnd)
        assert cur + size <= self.n, (
            f"arena overflow: need {size} at {cur}, cap {self.n}; "
            f"map={sorted((o, s, k) for k, (o, s, _, _, _) in self.regions.items())}")
        return cur

    def _fit_hi(self, size):
        taken = sorted(((o, o + s) for (o, s, _, _, _) in self.regions.values()), reverse=True)
        cur = self.n
        for a, bnd in taken:
            if cur - bnd >= size:
                return ((cur - size) // 64) * 64
            cur = min(cur, a)
        assert cur - size >= 0, (
            f"arena overflow(hi): need {size} below {cur}; "
            f"map={sorted((o, s, k) for k, (o, s, _, _, _) in self.regions.items())}")
        return ((cur - size) // 64) * 64

    def subbufs(self, parent, n, name):
        out = []
        for i in range(n):
            b = self.mk.buf(f"{name}{i}")
            if parent.w:
                b.rs.append(parent.w)
            b.rs.extend(parent.rs)
            out.append(b)
        return out

    def free(self, *names):
        for name in names:
            ks = [k for k in self.regions if k == name or
                  (k.startswith(name) and k[len(name):].isdigit())]
            for k in ks:
                o, s, b, _, _ = self.regions[k]
                self.ghosts.append((o, s, b))
                del self.regions[k]



UPTO = "G"
ROPE_ON = True
GLOBROPE_ON = True


def build_nc():
    nc = bass.Bass("TRN2", target_bir_lowering=False, debug=False)
    dram = lambda n, sh, dt=F32, kind="ExternalInput": nc.dram_tensor(n, sh, dt, kind=kind).ap()

    x_win = dram("x_win", [D, WIN])
    x_glob = dram("x_glob", [128, NCD * NGL])
    wq_h = dram("wq", [NH, 128, D])
    wkv_h = dram("wkv", [NKV, 128, D])
    wku_h = dram("wku", [128, 2048])
    wvu_h = dram("wvu", [128, 2048])
    wo_h = dram("wo", [NM, 128, D])
    gate_h = dram("gate", [128, NCD * NE])
    wg_h = dram("wg", [NE, NJ, 128, D])
    wd_h = dram("wd", [NGRP, NM, 128, EG * NDJ * 128])
    sim_h = dram("simm", [32, 32])
    mc_h = dram("mc", [128, MCTOT], BF16)
    maskg_h = dram("maskg", [32, NQT])
    ropew_h = dram("ropew", [2, 128, WIN])
    ropewl_h = dram("ropewl", [2, 128, WIN], BF16)
    ropeg_h = dram("ropeg", [2, 128, NGL])
    ropegl_h = dram("ropegl", [2, 128, NGL], BF16)
    iota_h = dram("iota", [128, 32])
    ident_h = dram("ident", [128, 128])
    sel_h = dram("sel", [32, NE * 128], BF16)
    out_h = dram("out", [D, NTQ], kind="ExternalOutput")

    st = contextlib.ExitStack()
    with st:
        ART = st.enter_context(nc.sbuf_tensor("arena", [128, 212800], U8))
        P = [st.enter_context(nc.psum_tensor(f"P{i}", [128, 1024], F32)) for i in range(4)]
        mk = MK(nc)
        A = Arena(mk, ART, 212800)
        bP = [mk.buf(f"P{i}", excl=True) for i in range(4)]
        bOut = [mk.buf(f"out{c}") for c in range(NCD)]

        def _trunc(phase):
            if UPTO != phase:
                return False
            dmy, b_dmy = A.alloc("dmy", 1024, F32)
            mk.op("dve", lambda E: E.memset(dmy, 0.0), writes=[b_dmy])
            for c in range(NCD):
                mk.op("pool", lambda E, c=c: E.dma_start(out=out_h[c * 128:(c + 1) * 128, :], in_=dmy),
                      reads=[b_dmy], writes=[bOut[c]], dma=True)
            mk.wait_all("pool", bOut)
            mk.finish(st)
            return True


        k_loc, b_k = A.alloc("k", NKV * WIN, BF16)
        v_km, b_v = A.alloc("v", NKV * WIN, BF16)
        kg_sb, b_kg = A.alloc("kg", NKV * NGL, BF16)
        vg_sb, b_vg = A.alloc("vg", NKV * 128, BF16)
        cb_sb, b_cb = A.alloc("cb", NH, F32)
        ones_bf, b_ones = A.alloc("ones", 128, BF16)
        mk.op("dve", lambda E: E.memset(ones_bf, 1.0), writes=[b_ones])
        eps_sb, b_eps = A.alloc("eps", 1, F32)
        mk.op("dve", lambda E: E.memset(eps_sb, 1e-5), writes=[b_eps])
        if _trunc("Z"):
            return nc

        wst, b_wst = A.alloc("wst", 2048, F32, nbufs=2)
        wbf, b_wbf = A.alloc("wbf", 2048, BF16, nbufs=2)


        ropec, b_ropec = A.alloc("ropec", 2 * WIN, F32, side="hi")
        ropecl, b_ropecl = A.alloc("ropecl", 2 * WIN, BF16, side="hi")
        ropeg, b_ropeg = A.alloc("ropeg", 2 * NGL, F32, side="hi")
        ropegl, b_ropegl = A.alloc("ropegl", 2 * NGL, BF16, side="hi")
        mk.op("sp", lambda E: E.dma_start(out=ropec.rearrange("p (a f) -> p a f", a=2), in_=ropew_h.rearrange("a p f -> p a f")),
              writes=[b_ropec], dma=True)
        mk.op("sp", lambda E: E.dma_start(out=ropecl.rearrange("p (a f) -> p a f", a=2), in_=ropewl_h.rearrange("a p f -> p a f")),
              writes=[b_ropecl], dma=True)
        mk.op("sp", lambda E: E.dma_start(out=ropeg.rearrange("p (a f) -> p a f", a=2), in_=ropeg_h.rearrange("a p f -> p a f")),
              writes=[b_ropeg], dma=True)
        if False:
            pass
        mk.op("sp", lambda E: E.dma_start(out=ropegl.rearrange("p (a f) -> p a f", a=2), in_=ropegl_h.rearrange("a p f -> p a f")),
              writes=[b_ropegl], dma=True)

        xn1, b_xn1 = A.alloc("xn1", NCD * WIN, BF16, side="hi")
        b_xn1c = A.subbufs(b_xn1, NCD, "xn1_")
        ckv, b_ckv = A.alloc("ckv", 4 * (WIN + NGL), BF16, side="hi")
        b_ckvc = A.subbufs(b_ckv, 4, "ckv_")
        xn1g_all, b_xn1g_all = A.alloc("xn1ga", NGL * NCD, BF16, side="hi")
        sqt, b_sqt = A.alloc("sqt", WIN, BF16, nbufs=2, side="hi")
        sqg, b_sqg = A.alloc("sqg", NGL, BF16, side="hi")
        sqg2, b_sqg2 = A.alloc("sqg2", NGL * NCD, BF16, side="hi")
        ab_bc, b_ab = A.alloc("ab", 2 * WIN, F32, side="hi")
        abg, b_abg = A.alloc("abg", 2 * NGL, F32, side="hi")
        xst, b_xst = A.alloc("xst", WIN, F32, nbufs=2, side="hi")
        xbfT, b_xbfT = A.alloc("xbfT", WIN, BF16, nbufs=2, side="hi")
        xgst, b_xgst = A.alloc("xgst", NGL * NCD, F32, side="hi")
        xgbf, b_xgbf = A.alloc("xgbf", NGL * NCD, BF16, side="hi")

        if _trunc("A0"):
            return nc
        for c in range(NCD):
            par = c % 2
            mk.op("pool", lambda E, c=c, par=par: E.dma_start(out=xst[par], in_=x_win[c * 128:(c + 1) * 128, :]),
                  writes=[b_xst[par]], dma=True)
            mk.op("dve", lambda E, par=par: E.tensor_copy(out=xbfT[par], in_=xst[par]),
                  reads=[b_xst[par]], writes=[b_xbfT[par]])
            mk.op("act", lambda E, par=par: E.square(out=sqt[par], in_=xbfT[par]),
                  reads=[b_xbfT[par]], writes=[b_sqt[par]])
            for i, (src, bsrc, pbase) in enumerate([(xbfT[par], b_xbfT[par], 0), (sqt[par], b_sqt[par], 2)]):
                for s3 in range(3):
                    pt = P[pbase] if s3 < 2 else P[pbase + 1]
                    bpt = bP[pbase] if s3 < 2 else bP[pbase + 1]
                    cl = (s3 % 2) * 512
                    last = (c == NCD - 1)
                    first = (c == 0)
                    rnd_end = (i == 1 and s3 == 2)
                    mk.op("pe", lambda E, pt=pt, src=src, s3=s3, cl=cl, c=c: E.matmul(
                            out=pt[:, cl:cl + 512], lhsT=ones_bf, rhs=src[:, s3 * 512:(s3 + 1) * 512],
                            start=(c == 0), stop=(c == NCD - 1)),
                          reads=[bsrc, b_ones], writes=[bpt] if (first or last) else [],
                          inc=(last or rnd_end))
        mk.op("pool", lambda E: E.dma_start(out=xgst, in_=x_glob), writes=[b_xgst], dma=True)
        mk.op("dve", lambda E: E.tensor_copy(out=xgbf, in_=xgst), reads=[b_xgst], writes=[b_xgbf])

        if _trunc("A1"):
            return nc
        tmp1, b_t1 = A.alloc("Atmp1", WIN, F32, side="hi")
        tmp2, b_t2 = A.alloc("Atmp2", WIN, F32, side="hi")
        for (pm, pv, cl, off, w) in [(0, 2, 0, 0, 1024), (1, 3, 0, 1024, 512)]:
            mk.op("dve", lambda E, pm=pm, cl=cl, off=off, w=w: E.tensor_scalar_mul(
                    out=tmp1[:, off:off + w], in0=P[pm][:, cl:cl + w], scalar1=1.0 / D),
                  reads=[bP[pm]], writes=[b_t1])
            mk.op("dve", lambda E, pv=pv, cl=cl, off=off, w=w: E.tensor_scalar_mul(
                    out=tmp2[:, off:off + w], in0=P[pv][:, cl:cl + w], scalar1=1.0 / D),
                  reads=[bP[pv]], writes=[b_t2])
        mk.op("dve", lambda E: E.tensor_tensor(out=ab_bc[:, 0:WIN], in0=tmp1, in1=tmp1, op=ALU.mult),
              reads=[b_t1], writes=[b_ab])
        mk.op("dve", lambda E: E.tensor_tensor(out=tmp2, in0=tmp2, in1=ab_bc[:, 0:WIN], op=ALU.subtract),
              reads=[b_t2, b_ab], writes=[b_t2])
        mk.op("act", lambda E: E.activation(out=tmp2, in_=tmp2, func=AF.Sqrt, bias=eps_sb),
              reads=[b_t2, b_eps], writes=[b_t2])
        mk.op("dve", lambda E: E.reciprocal(out=ab_bc[:, 0:WIN], in_=tmp2), reads=[b_t2], writes=[b_ab])
        mk.op("dve", lambda E: E.tensor_tensor(out=ab_bc[:, WIN:2 * WIN], in0=tmp1,
                                               in1=ab_bc[:, 0:WIN], op=ALU.mult),
              reads=[b_t1, b_ab], writes=[b_ab])
        mk.op("dve", lambda E: E.tensor_scalar_mul(out=ab_bc[:, WIN:2 * WIN],
                                                   in0=ab_bc[:, WIN:2 * WIN], scalar1=-1.0),
              reads=[b_ab], writes=[b_ab])

        if _trunc("A2"):
            return nc
        for c in range(NCD):
            last = (c == NCD - 1)
            mk.op("pe", lambda E, c=c, last=last: E.matmul(out=P[0][:, 0:NGL], lhsT=ones_bf,
                                                rhs=xgbf[:, c * NGL:(c + 1) * NGL],
                                                start=(c == 0), stop=last),
                  reads=[b_xgbf, b_ones], writes=[bP[0]] if (c == 0 or last) else [], inc=last)
        mk.op("act", lambda E: E.square(out=sqg2, in_=xgbf), reads=[b_xgbf], writes=[b_sqg2])
        for c in range(NCD):
            last = (c == NCD - 1)
            mk.op("pe", lambda E, c=c, last=last: E.matmul(out=P[2][:, 0:NGL], lhsT=ones_bf,
                                                rhs=sqg2[:, c * NGL:(c + 1) * NGL],
                                                start=(c == 0), stop=last),
                  reads=[b_sqg2, b_ones], writes=[bP[2]] if (c == 0 or last) else [], inc=last)
        gt1, b_gt1 = A.alloc("gt1", NGL, F32, side="hi")
        gt2, b_gt2 = A.alloc("gt2", NGL, F32, side="hi")
        mk.op("dve", lambda E: E.tensor_scalar_mul(out=gt1, in0=P[0][:, 0:NGL], scalar1=1.0 / D),
              reads=[bP[0]], writes=[b_gt1])
        mk.op("dve", lambda E: E.tensor_scalar_mul(out=gt2, in0=P[2][:, 0:NGL], scalar1=1.0 / D),
              reads=[bP[2]], writes=[b_gt2])
        mk.op("dve", lambda E: E.tensor_tensor(out=abg[:, 0:NGL], in0=gt1, in1=gt1, op=ALU.mult),
              reads=[b_gt1], writes=[b_abg])
        mk.op("dve", lambda E: E.tensor_tensor(out=gt2, in0=gt2, in1=abg[:, 0:NGL], op=ALU.subtract),
              reads=[b_gt2, b_abg], writes=[b_gt2])
        mk.op("act", lambda E: E.activation(out=gt2, in_=gt2, func=AF.Sqrt, bias=eps_sb),
              reads=[b_gt2, b_eps], writes=[b_gt2])
        mk.op("dve", lambda E: E.reciprocal(out=abg[:, 0:NGL], in_=gt2), reads=[b_gt2], writes=[b_abg])
        mk.op("dve", lambda E: E.tensor_tensor(out=abg[:, NGL:2 * NGL], in0=gt1,
                                               in1=abg[:, 0:NGL], op=ALU.mult),
              reads=[b_gt1, b_abg], writes=[b_abg])
        mk.op("dve", lambda E: E.tensor_scalar_mul(out=abg[:, NGL:2 * NGL],
                                                   in0=abg[:, NGL:2 * NGL], scalar1=-1.0),
              reads=[b_abg], writes=[b_abg])

        for c in range(NCD):
            par = c % 2
            mk.op("pool", lambda E, c=c, par=par: E.dma_start(out=xst[par], in_=x_win[c * 128:(c + 1) * 128, :]),
                  writes=[b_xst[par]], dma=True)
            mk.op("dve", lambda E, par=par: E.tensor_tensor(out=tmp1, in0=xst[par],
                                                            in1=ab_bc[:, 0:WIN], op=ALU.mult),
                  reads=[b_xst[par], b_ab], writes=[b_t1])
            mk.op("dve", lambda E, c=c: E.tensor_tensor(out=xn1[:, c * WIN:(c + 1) * WIN], in0=tmp1,
                                                        in1=ab_bc[:, WIN:2 * WIN], op=ALU.add),
                  reads=[b_t1, b_ab], writes=[b_xn1c[c]])
        b_xn1gc = A.subbufs(b_xn1g_all, NCD, "xn1g_")
        for c in range(NCD):
            mk.op("dve", lambda E, c=c: E.tensor_tensor(out=gt1, in0=xgst[:, c * NGL:(c + 1) * NGL],
                                                        in1=abg[:, 0:NGL], op=ALU.mult),
                  reads=[b_xgst, b_abg], writes=[b_gt1])
            mk.op("dve", lambda E, c=c: E.tensor_tensor(out=xn1g_all[:, c * NGL:(c + 1) * NGL],
                                                        in0=gt1, in1=abg[:, NGL:2 * NGL], op=ALU.add),
                  reads=[b_gt1, b_abg], writes=[b_xn1gc[c]])

        if _trunc("A3"):
            return nc
        A.free("ab", "abg", "xst", "xbfT", "xgst", "xgbf", "Atmp1", "Atmp2",
               "gt1", "gt2")
        q_fm, b_q = A.alloc("q", NH * NTQ, BF16)

        for m in range(NKV):
            par = m % 2
            mk.op("sp", lambda E, m=m, par=par: E.dma_start(out=wst[par], in_=wkv_h[m]),
                  writes=[b_wst[par]], dma=True)
            mk.op("dve", lambda E, par=par: E.tensor_copy(out=wbf[par], in_=wst[par]),
                  reads=[b_wst[par]], writes=[b_wbf[par]])
            pw = P[par]
            pw2 = P[par + 2]
            for c in range(NCD):
                for s3 in range(3):
                    pt = pw if s3 < 2 else pw2
                    bpt = bP[par] if s3 < 2 else bP[par + 2]
                    cl = (s3 % 2) * 512
                    first = (c == 0)
                    mk.op("pe", lambda E, pt=pt, par=par, c=c, s3=s3, cl=cl: E.matmul(
                            out=pt[:, cl:cl + 512], lhsT=wbf[par][:, c * 128:(c + 1) * 128],
                            rhs=xn1[:, c * WIN + s3 * 512: c * WIN + s3 * 512 + 512],
                            start=(c == 0), stop=(c == NCD - 1)),
                          reads=[b_wbf[par], b_xn1c[c]],
                          writes=[bpt] if first else [], inc=False)
                lastg = (c == NCD - 1)
                mk.op("pe", lambda E, pw2=pw2, par=par, c=c: E.matmul(
                        out=pw2[:, 512:512 + NGL], lhsT=wbf[par][:, c * 128:(c + 1) * 128],
                        rhs=xn1g_all[:, c * NGL:(c + 1) * NGL],
                        start=(c == 0), stop=(c == NCD - 1)),
                      reads=[b_wbf[par], b_xn1gc[c]],
                      writes=[bP[par], bP[par + 2]] if lastg else [], inc=lastg)
            mk.op("act", lambda E, m=m, pw=pw: E.copy(out=ckv[:, m * (WIN + NGL): m * (WIN + NGL) + 1024],
                                                      in_=pw[:, 0:1024]),
                  reads=[bP[par]], writes=[b_ckvc[m]])
            mk.op("act", lambda E, m=m, pw2=pw2: E.copy(
                    out=ckv[:, m * (WIN + NGL) + 1024: m * (WIN + NGL) + WIN + NGL],
                    in_=pw2[:, 0:512 + NGL]),
                  reads=[bP[par + 2]], writes=[b_ckvc[m]])

        if _trunc("A4"):
            return nc
        wkub, b_wkub = A.alloc("wkub", 4 * 512, BF16, side="hi")
        mk.op("sp", lambda E: E.dma_start(out=wst[0], in_=wku_h), writes=[b_wst[0]], dma=True)
        mk.op("dve", lambda E: E.tensor_copy(out=wkub, in_=wst[0]), reads=[b_wst[0]], writes=[b_wkub])
        wvub, b_wvub = A.alloc("wvub", 4 * 512, BF16, side="hi")
        mk.op("sp", lambda E: E.dma_start(out=wst[1], in_=wvu_h), writes=[b_wst[1]], dma=True)
        mk.op("dve", lambda E: E.tensor_copy(out=wvub, in_=wst[1]), reads=[b_wst[1]], writes=[b_wvub])

        rot, b_rot = A.alloc("rot", WIN, F32, side="hi")
        rtmp, b_rtmp = A.alloc("rtmp", WIN, F32, side="hi")
        rtmp2, b_rtmp2 = A.alloc("rtmp2", WIN, F32, side="hi")

        def rope_apply(psum_t, b_ps, width, cos_hi, sin_hi, cos_lo, sin_lo, b_tabs, outap, b_out, extra_reads=()):
            if not ROPE_ON:
                mk.op("act", lambda E: E.copy(out=outap, in_=psum_t[:, 0:width]),
                      reads=[b_ps], writes=[b_out])
                return
            mk.op("act", lambda E: E.activation(out=rot[0:64, 0:width], in_=psum_t[64:128, 0:width],
                                                func=AF.Copy, scale=-1.0),
                  reads=[b_ps], writes=[b_rot])
            mk.op("act", lambda E: E.copy(out=rot[64:128, 0:width], in_=psum_t[0:64, 0:width]),
                  reads=[b_ps], writes=[b_rot])
            mk.op("dve", lambda E: E.tensor_tensor(out=rtmp[:, 0:width], in0=psum_t[:, 0:width],
                                                   in1=cos_hi, op=ALU.mult),
                  reads=[b_ps] + list(b_tabs), writes=[b_rtmp])
            mk.op("dve", lambda E: E.tensor_tensor(out=rtmp2[:, 0:width], in0=rot[:, 0:width],
                                                   in1=sin_hi, op=ALU.mult),
                  reads=[b_rot] + list(b_tabs), writes=[b_rtmp2])
            mk.op("dve", lambda E: E.tensor_tensor(out=rtmp[:, 0:width], in0=rtmp[:, 0:width],
                                                   in1=rtmp2[:, 0:width], op=ALU.add),
                  reads=[b_rtmp2, b_rtmp], writes=[b_rtmp])
            mk.op("dve", lambda E: E.tensor_tensor(out=rtmp2[:, 0:width], in0=psum_t[:, 0:width],
                                                   in1=cos_lo, op=ALU.mult),
                  reads=[b_ps] + list(b_tabs), writes=[b_rtmp2])
            mk.op("dve", lambda E: E.tensor_tensor(out=rtmp[:, 0:width], in0=rtmp[:, 0:width],
                                                   in1=rtmp2[:, 0:width], op=ALU.add),
                  reads=[b_rtmp2, b_rtmp], writes=[b_rtmp])
            mk.op("dve", lambda E: E.tensor_tensor(out=rtmp2[:, 0:width], in0=rot[:, 0:width],
                                                   in1=sin_lo, op=ALU.mult),
                  reads=[b_rot] + list(b_tabs), writes=[b_rtmp2])
            mk.op("dve", lambda E: E.tensor_tensor(out=outap, in0=rtmp[:, 0:width],
                                                   in1=rtmp2[:, 0:width], op=ALU.add),
                  reads=[b_rtmp2, b_rtmp] + list(extra_reads), writes=[b_out])

        for m in range(NKV):
            par = m % 2
            pw, pw2 = P[par], P[par + 2]
            for c in range(4):
                for s3 in range(3):
                    pt = pw if s3 < 2 else pw2
                    cl = (s3 % 2) * 512
                    mk.op("pe", lambda E, pt=pt, m=m, c=c, s3=s3, cl=cl: E.matmul(
                            out=pt[:, cl:cl + 512], lhsT=wkub[:, c * 512 + m * 128: c * 512 + m * 128 + 128],
                            rhs=ckv[:, c * (WIN + NGL) + s3 * 512: c * (WIN + NGL) + s3 * 512 + 512],
                            start=(c == 0), stop=(c == 3)),
                          reads=[b_wkub, b_ckvc[c]],
                          writes=[bP[par] if s3 < 2 else bP[par + 2]] if c == 0 else [], inc=False)
                mk.op("pe", lambda E, pw2=pw2, m=m, c=c: E.matmul(
                        out=pw2[:, 512:512 + NGL],
                        lhsT=wkub[:, c * 512 + m * 128: c * 512 + m * 128 + 128],
                        rhs=ckv[:, c * (WIN + NGL) + WIN: c * (WIN + NGL) + WIN + NGL],
                        start=(c == 0), stop=(c == 3)),
                      reads=[b_wkub, b_ckvc[c]],
                      writes=[bP[par], bP[par + 2]] if c == 3 else [], inc=(c == 3))
            rope_apply(pw, bP[par], 1024,
                       ropec[:, 0:1024], ropec[:, WIN:WIN + 1024],
                       ropecl[:, 0:1024], ropecl[:, WIN:WIN + 1024], [b_ropec, b_ropecl],
                       k_loc[:, m * WIN: m * WIN + 1024], b_k)
            rope_apply(pw2, bP[par + 2], 512,
                       ropec[:, 1024:WIN], ropec[:, WIN + 1024:2 * WIN],
                       ropecl[:, 1024:WIN], ropecl[:, WIN + 1024:2 * WIN], [b_ropec, b_ropecl],
                       k_loc[:, m * WIN + 1024: m * WIN + WIN], b_k)
            if not GLOBROPE_ON:
                mk.op("act", lambda E, pw2=pw2, m=m: E.copy(out=kg_sb[:, m * NGL:(m + 1) * NGL],
                                                            in_=pw2[:, 512:512 + NGL]),
                      reads=[bP[par + 2]], writes=[b_kg])
                continue
            mk.op("act", lambda E, pw2=pw2: E.activation(out=rot[0:64, 0:NGL], in_=pw2[64:128, 512:512 + NGL],
                                                         func=AF.Copy, scale=-1.0),
                  reads=[bP[par + 2]], writes=[b_rot])
            mk.op("act", lambda E, pw2=pw2: E.copy(out=rot[64:128, 0:NGL], in_=pw2[0:64, 512:512 + NGL]),
                  reads=[bP[par + 2]], writes=[b_rot])
            mk.op("dve", lambda E, pw2=pw2: E.tensor_tensor(out=rtmp[:, 0:NGL], in0=pw2[:, 512:512 + NGL],
                                                            in1=ropeg[:, 0:NGL], op=ALU.mult),
                  reads=[bP[par + 2], b_ropeg], writes=[b_rtmp])
            mk.op("dve", lambda E: E.tensor_tensor(out=rtmp2[:, 0:NGL], in0=rot[:, 0:NGL],
                                                   in1=ropeg[:, NGL:2 * NGL], op=ALU.mult),
                  reads=[b_rot, b_ropeg], writes=[b_rtmp2])
            mk.op("dve", lambda E: E.tensor_tensor(out=rtmp[:, 0:NGL], in0=rtmp[:, 0:NGL],
                                                   in1=rtmp2[:, 0:NGL], op=ALU.add),
                  reads=[b_rtmp2, b_rtmp], writes=[b_rtmp])
            mk.op("dve", lambda E, pw2=pw2: E.tensor_tensor(out=rtmp2[:, 0:NGL], in0=pw2[:, 512:512 + NGL],
                                                            in1=ropegl[:, 0:NGL], op=ALU.mult),
                  reads=[bP[par + 2], b_ropegl], writes=[b_rtmp2])
            mk.op("dve", lambda E: E.tensor_tensor(out=rtmp[:, 0:NGL], in0=rtmp[:, 0:NGL],
                                                   in1=rtmp2[:, 0:NGL], op=ALU.add),
                  reads=[b_rtmp2, b_rtmp], writes=[b_rtmp])
            mk.op("dve", lambda E: E.tensor_tensor(out=rtmp2[:, 0:NGL], in0=rot[:, 0:NGL],
                                                   in1=ropegl[:, NGL:2 * NGL], op=ALU.mult),
                  reads=[b_rot, b_ropegl], writes=[b_rtmp2])
            mk.op("dve", lambda E, m=m: E.tensor_tensor(out=kg_sb[:, m * NGL:(m + 1) * NGL],
                                                        in0=rtmp[:, 0:NGL], in1=rtmp2[:, 0:NGL], op=ALU.add),
                  reads=[b_rtmp2, b_rtmp], writes=[b_kg])

        if _trunc("A5"):
            return nc
        for t in range(NWC):
            par = t % 2
            pt = P[par]
            for c in range(4):
                last = (c == 3)
                mk.op("pe", lambda E, pt=pt, t=t, c=c, last=last: E.matmul(
                        out=pt[:, 0:512],
                        lhsT=ckv[:, c * (WIN + NGL) + t * 128: c * (WIN + NGL) + t * 128 + 128],
                        rhs=wvub[:, c * 512:(c + 1) * 512],
                        start=(c == 0), stop=last),
                      reads=[b_ckvc[c], b_wvub], writes=[bP[par]] if (c == 0 or last) else [], inc=last)
            mk.op("act", lambda E, pt=pt, t=t: E.copy(
                    out=v_km.rearrange("p (m f) -> p m f", m=NKV)[:, :, t * 128:(t + 1) * 128],
                    in_=pt[:, 0:512].rearrange("p (m f) -> p m f", m=NKV)),
                  reads=[bP[par]], writes=[b_v])
        for c in range(4):
            last = (c == 3)
            mk.op("pe", lambda E, c=c, last=last: E.matmul(
                    out=P[2][:, 0:512][0:NGL, :],
                    lhsT=ckv[:, c * (WIN + NGL) + WIN: c * (WIN + NGL) + WIN + NGL],
                    rhs=wvub[:, c * 512:(c + 1) * 512],
                    start=(c == 0), stop=last),
                  reads=[b_ckvc[c], b_wvub], writes=[bP[2]] if (c == 0 or last) else [], inc=last)
        mk.op("act", lambda E: E.copy(out=vg_sb.rearrange("p (m f) -> p m f", m=NKV)[0:NGL, :, :],
                                      in_=P[2][0:NGL, 0:512].rearrange("p (m f) -> p m f", m=NKV)),
              reads=[bP[2]], writes=[b_vg])

        if _trunc("A6"):
            return nc
        for m in range(NH):
            par = m % 2
            mk.op("sp", lambda E, m=m, par=par: E.dma_start(out=wst[par], in_=wq_h[m]),
                  writes=[b_wst[par]], dma=True)
            mk.op("dve", lambda E, par=par: E.tensor_copy(out=wbf[par], in_=wst[par]),
                  reads=[b_wst[par]], writes=[b_wbf[par]])
            pq = P[par]
            for c in range(NCD):
                for s2 in range(2):
                    first = (c == 0)
                    last = (c == NCD - 1 and s2 == 1)
                    mk.op("pe", lambda E, pq=pq, par=par, c=c, s2=s2: E.matmul(
                            out=pq[:, s2 * 512:(s2 + 1) * 512],
                            lhsT=wbf[par][:, c * 128:(c + 1) * 128],
                            rhs=xn1[:, c * WIN + 512 + s2 * 512: c * WIN + 512 + s2 * 512 + 512],
                            start=(c == 0), stop=(c == NCD - 1)),
                          reads=[b_wbf[par], b_xn1c[c]],
                          writes=[bP[par]] if (first or last) else [], inc=last)
            rope_apply(pq, bP[par], 1024,
                       ropec[:, 512:WIN], ropec[:, WIN + 512:2 * WIN],
                       ropecl[:, 512:WIN], ropecl[:, WIN + 512:2 * WIN], [b_ropec, b_ropecl],
                       q_fm[:, m * NTQ:(m + 1) * NTQ], b_q)

        if _trunc("A7"):
            return nc
        mx_sb, b_mx = A.alloc("mx", 2 * 32, F32, side="hi")
        for g in range(NKV):
            mk.op("act", lambda E, g=g: E.square(out=sqt[0], in_=k_loc[:, g * WIN:(g + 1) * WIN]),
                  reads=[b_k], writes=[b_sqt[0]])
            for s3 in range(3):
                last = (s3 == 2)
                mk.op("pe", lambda E, s3=s3, last=last: E.matmul(out=P[0][:, 0:512],
                                                      lhsT=ones_bf, rhs=sqt[0][:, s3 * 512:(s3 + 1) * 512],
                                                      start=(s3 == 0), stop=last),
                      reads=[b_sqt[0], b_ones], writes=[bP[0]] if (s3 == 0 or last) else [], inc=last)
            mk.op("dve", lambda E, g=g: E.reduce_max(out=mx_sb[:, 16 + g:17 + g], in_=P[0][:, 0:512],
                                                     axis=AX.X),
                  reads=[bP[0]], writes=[b_mx])
        for h in range(NH):
            mk.op("act", lambda E, h=h: E.square(out=sqt[1][:, 0:1024], in_=q_fm[:, h * NTQ:(h + 1) * NTQ]),
                  reads=[b_q], writes=[b_sqt[1]])
            for s2 in range(2):
                last = (s2 == 1)
                mk.op("pe", lambda E, s2=s2, last=last: E.matmul(out=P[1][:, 0:512], lhsT=ones_bf,
                                                      rhs=sqt[1][:, s2 * 512:(s2 + 1) * 512],
                                                      start=(s2 == 0), stop=last),
                      reads=[b_sqt[1], b_ones], writes=[bP[1]] if (s2 == 0 or last) else [], inc=last)
            mk.op("dve", lambda E, h=h: E.reduce_max(out=mx_sb[:, h:h + 1], in_=P[1][:, 0:512], axis=AX.X),
                  reads=[bP[1]], writes=[b_mx])
        for g in range(NKV):
            mk.op("act", lambda E, g=g: E.square(out=sqg, in_=kg_sb[:, g * NGL:(g + 1) * NGL]),
                  reads=[b_kg], writes=[b_sqg])
            last = True
            mk.op("pe", lambda E: E.matmul(out=P[0][:, 0:NGL], lhsT=ones_bf, rhs=sqg,
                                           start=True, stop=True),
                  reads=[b_sqg, b_ones], writes=[bP[0]], inc=True)
            mk.op("dve", lambda E, g=g: E.reduce_max(out=mx_sb[:, 24 + g:25 + g], in_=P[0][:, 0:NGL],
                                                     axis=AX.X),
                  reads=[bP[0]], writes=[b_mx])
        mk.op("dve", lambda E: E.tensor_tensor(out=mx_sb[:, 16:20], in0=mx_sb[:, 16:20],
                                               in1=mx_sb[:, 24:28], op=ALU.max),
              reads=[b_mx], writes=[b_mx])
        for h in range(NH):
            mk.op("dve", lambda E, h=h: E.tensor_tensor(out=mx_sb[:, 32 + h:33 + h], in0=mx_sb[:, h:h + 1],
                                                        in1=mx_sb[:, 16 + h // 4:17 + h // 4], op=ALU.mult),
                  reads=[b_mx], writes=[b_mx])
        mk.op("act", lambda E: E.sqrt(out=mx_sb[:, 32:48], in_=mx_sb[:, 32:48]),
              reads=[b_mx], writes=[b_mx])
        mk.op("dve", lambda E: E.tensor_scalar_mul(out=cb_sb, in0=mx_sb[:, 32:48], scalar1=-SCALE),
              reads=[b_mx], writes=[b_cb])

        A.free("ropec", "ropecl", "ropeg", "ropegl", "xn1", "ckv",
               "sqt", "sqg", "sqg2",
               "wkub", "wvub", "rot", "rtmp", "rtmp2", "mx", "xn1ga")

        if _trunc("A"):
            return nc
        mc_sb, b_mc = A.alloc("mcsb", MCTOT, BF16, side="hi")
        mk.op("sp", lambda E: E.dma_start(out=mc_sb, in_=mc_h), writes=[b_mc], dma=True)
        mg_sb, b_mg = A.alloc("mgsb", NQT, F32, side="hi")
        mk.op("sp", lambda E: E.dma_start(out=mg_sb[0:32, :], in_=maskg_h), writes=[b_mg], dma=True)
        y_all, b_y = A.alloc("yall", NH * NTQ, BF16)
        PT_sb, b_ptv = A.alloc("pt", 4 * 6 * 640, BF16, side="hi")
        b_pts = [A.subbufs(b_ptv, 6, f"pt{h}_") for h in range(4)]
        pg_sb, b_pg = A.alloc("pg", 512, BF16, side="hi")
        rec_sb, b_rec = A.alloc("rec", 512, F32, nbufs=2, side="hi")

        for g in range(4):
            qk_i = 0
            for kk in range(NWC):
                if True:
                    W = WK[kk]
                    for hl in range(4):
                        h = 4 * g + hl
                        pq = P[2 + (qk_i % 2)]
                        bpq = bP[2 + (qk_i % 2)]
                        qk_i += 1
                        nmm = (W + 511) // 512
                        for s in range(nmm):
                            w0 = s * 512
                            wlen = min(512, W - w0)
                            first = (s == 0)
                            last = (s == nmm - 1)
                            mk.op("pe", lambda E, pq=pq, g=g, kk=kk, h=h, w0=w0, wlen=wlen: E.matmul(
                                    out=pq[:, w0:w0 + wlen],
                                    lhsT=k_loc[:, g * WIN + kk * 128: g * WIN + kk * 128 + 128],
                                    rhs=q_fm[:, h * NTQ + QB[kk] + w0: h * NTQ + QB[kk] + w0 + wlen],
                                    start=True, stop=True),
                                  reads=[b_k, b_q], writes=[bpq] if (first or last) else [],
                                  inc=last)
                        slot = kk % 6
                        pts = b_pts[hl][slot]
                        mk.op("act", lambda E, pq=pq, hl=hl, slot=slot, W=W, h=h: E.activation(
                                out=PT_sb[:, hl * 3840 + slot * 640: hl * 3840 + slot * 640 + W],
                                in_=pq[:, 0:W], func=AF.Exp, bias=cb_sb[:, h:h + 1], scale=SCALE),
                              reads=[bpq, b_cb], writes=[pts])
                        mk.op("dve", lambda E, hl=hl, slot=slot, kk=kk, W=W: E.tensor_tensor(
                                out=PT_sb[:, hl * 3840 + slot * 640: hl * 3840 + slot * 640 + W],
                                in0=PT_sb[:, hl * 3840 + slot * 640: hl * 3840 + slot * 640 + W],
                                in1=mc_sb[:, MCOFF[kk]:MCOFF[kk] + W], op=ALU.mult),
                              reads=[pts, b_mc], writes=[pts])
                if kk >= 4:
                    LT = kk - 4
                    assert 0 <= LT < NQT
                    ys = P[LT % 2]
                    bys = bP[LT % 2]
                    for hl in range(4):
                        h = 4 * g + hl
                        mk.op("pe", lambda E, ys=ys, g=g, hl=hl, LT=LT: E.matmul(
                                out=ys[0:NGL, hl * 128:(hl + 1) * 128],
                                lhsT=kg_sb[:, g * NGL:(g + 1) * NGL],
                                rhs=q_fm[:, (4 * g + hl) * NTQ + LT * 128: (4 * g + hl) * NTQ + LT * 128 + 128],
                                start=True, stop=True),
                              reads=[b_kg, b_q], writes=[bys])
                        mk.op("act", lambda E, ys=ys, hl=hl, h=h: E.activation(
                                out=pg_sb[0:NGL, hl * 128:(hl + 1) * 128],
                                in_=ys[0:NGL, hl * 128:(hl + 1) * 128],
                                func=AF.Exp, bias=cb_sb[0:NGL, h:h + 1], scale=SCALE),
                              reads=[bys, b_cb], writes=[b_pg])
                    mk.op("dve", lambda E, LT=LT: E.tensor_scalar(
                            out=pg_sb[0:NGL, :], in0=pg_sb[0:NGL, :],
                            scalar1=mg_sb[0:NGL, LT:LT + 1], scalar2=None, op0=ALU.mult),
                          reads=[b_pg, b_mg], writes=[b_pg])
                    pt_reads = []
                    for r in range(5):
                        kb = LT + r
                        slot = kb % 6
                        off = (LT - max(kb - 4, 0)) * 128
                        ap_cols = [(hl * 3840 + slot * 640 + off) for hl in range(4)]
                        pt_reads.extend(b_pts[hl][slot] for hl in range(4))
                        mk.op("pe", lambda E, ys=ys, g=g, kb=kb, slot=slot, off=off, r=r: E.matmul(
                                out=ys[:, 0:512],
                                lhsT=v_km[:, g * WIN + kb * 128: g * WIN + kb * 128 + 128],
                                rhs=PT_sb[:, :].rearrange("p (hl x) -> p hl x", hl=4)[:, :, slot * 640 + off: slot * 640 + off + 128],
                                start=(r == 0), stop=False),
                              reads=[b_v] + [b_pts[hl][slot] for hl in range(4)],
                              writes=[bys] if r == 0 else [], inc=False)
                    mk.op("pe", lambda E, ys=ys, g=g: E.matmul(
                            out=ys[:, 0:512],
                            lhsT=vg_sb.rearrange("p (m f) -> p m f", m=NKV)[0:NGL, g, :],
                            rhs=pg_sb[0:NGL, :], start=False, stop=True),
                          reads=[b_vg, b_pg], writes=[], inc=False)
                    for r in range(5):
                        kb = LT + r
                        slot = kb % 6
                        off = (LT - max(kb - 4, 0)) * 128
                        mk.op("pe", lambda E, ys=ys, slot=slot, off=off, r=r: E.matmul(
                                out=ys[:, 512:1024],
                                lhsT=ones_bf,
                                rhs=PT_sb[:, :].rearrange("p (hl x) -> p hl x", hl=4)[:, :, slot * 640 + off: slot * 640 + off + 128],
                                start=(r == 0), stop=False),
                              reads=[b_ones] + [b_pts[hl][slot] for hl in range(4)],
                              writes=[bys] if False else [], inc=False)
                    mk.op("pe", lambda E, ys=ys: E.matmul(
                            out=ys[:, 512:1024], lhsT=ones_bf[0:NGL, :], rhs=pg_sb[0:NGL, :],
                            start=False, stop=True),
                          reads=[b_ones, b_pg], writes=[bys], inc=True)
                    par = LT % 2
                    mk.op("dve", lambda E, ys=ys, par=par: E.reciprocal(out=rec_sb[par], in_=ys[:, 512:1024]),
                          reads=[bys], writes=[b_rec[par]])
                    mk.op("dve", lambda E, ys=ys, par=par, g=g, LT=LT: E.tensor_tensor(
                            out=y_all[:, :].rearrange("p (h x) -> p h x", h=NH)[
                                :, 4 * g:4 * g + 4, LT * 128:(LT + 1) * 128],
                            in0=ys[:, 0:512].rearrange("p (hl f) -> p hl f", hl=4),
                            in1=rec_sb[par].rearrange("p (hl f) -> p hl f", hl=4), op=ALU.mult),
                          reads=[bys, b_rec[par]], writes=[b_y])

        A.free("mcsb", "mgsb", "pt", "pg", "rec", "q", "k", "v", "kg", "vg", "cb")

        if _trunc("B"):
            return nc
        acc, b_accv = A.alloc("acc", NCD * NTQ, F32)
        b_acc = A.subbufs(b_accv, NM, "acc_")
        sq2, b_sq2 = A.alloc("sq2", NTQ, BF16, nbufs=2, side="hi")
        xb2, b_xb2 = A.alloc("xb2", NTQ, BF16, nbufs=2, side="hi")
        for c in range(NCD):
            mk.op("pool", lambda E, c=c: E.dma_start(out=acc[:, c * NTQ:(c + 1) * NTQ],
                                                     in_=x_win[c * 128:(c + 1) * 128, 512:WIN]),
                  writes=[b_acc[c]], dma=True)
        for m in range(NM):
            par = m % 2
            mk.op("sp", lambda E, m=m, par=par: E.dma_start(out=wst[par], in_=wo_h[m]),
                  writes=[b_wst[par]], dma=True)
            mk.op("dve", lambda E, par=par: E.tensor_copy(out=wbf[par], in_=wst[par]),
                  reads=[b_wst[par]], writes=[b_wbf[par]])
            po = P[par]
            for j in range(NH):
                for s2 in range(2):
                    first = (j == 0)
                    last = (j == NH - 1 and s2 == 1)
                    mk.op("pe", lambda E, po=po, par=par, j=j, s2=s2: E.matmul(
                            out=po[:, s2 * 512:(s2 + 1) * 512],
                            lhsT=wbf[par][:, j * 128:(j + 1) * 128],
                            rhs=y_all[:, j * NTQ + s2 * 512: j * NTQ + s2 * 512 + 512],
                            start=(j == 0), stop=(j == NH - 1)),
                          reads=[b_wbf[par], b_y], writes=[bP[par]] if (first or last) else [],
                          inc=last)
            mk.op("dve", lambda E, po=po, m=m: E.tensor_tensor(
                    out=acc[:, m * NTQ:(m + 1) * NTQ], in0=acc[:, m * NTQ:(m + 1) * NTQ],
                    in1=po[:, :], op=ALU.add),
                  reads=[bP[par], b_acc[m]], writes=[b_acc[m]])
            mk.op("dve", lambda E, m=m, par=par: E.tensor_copy(out=xb2[par], in_=acc[:, m * NTQ:(m + 1) * NTQ]),
                  reads=[b_acc[m]], writes=[b_xb2[par]])
            mk.op("act", lambda E, par=par: E.square(out=sq2[par], in_=xb2[par]),
                  reads=[b_xb2[par]], writes=[b_sq2[par]])
            for (src, bsrc, pi) in [(xb2[par], b_xb2[par], 2), (sq2[par], b_sq2[par], 3)]:
                for s2 in range(2):
                    first = (m == 0)
                    last = (m == NM - 1 and s2 == 1)
                    mk.op("pe", lambda E, src=src, pi=pi, s2=s2, m=m: E.matmul(
                            out=P[pi][:, s2 * 512:(s2 + 1) * 512], lhsT=ones_bf,
                            rhs=src[:, s2 * 512:(s2 + 1) * 512],
                            start=(m == 0), stop=(m == NM - 1)),
                          reads=[bsrc, b_ones], writes=[bP[pi]] if (first or last) else [],
                          inc=last)

        A.free("yall", "sq2", "xb2")

        if _trunc("C"):
            return nc
        ab2, b_ab2 = A.alloc("ab2", 2 * NTQ, F32, side="hi")
        dt1, b_dt1 = A.alloc("dt1", NTQ, F32, side="hi")
        dt2, b_dt2 = A.alloc("dt2", NTQ, F32, side="hi")
        mk.op("dve", lambda E: E.tensor_scalar_mul(out=dt1, in0=P[2][:, :], scalar1=1.0 / D),
              reads=[bP[2]], writes=[b_dt1])
        mk.op("dve", lambda E: E.tensor_scalar_mul(out=dt2, in0=P[3][:, :], scalar1=1.0 / D),
              reads=[bP[3]], writes=[b_dt2])
        mk.op("dve", lambda E: E.tensor_tensor(out=ab2[:, 0:NTQ], in0=dt1, in1=dt1, op=ALU.mult),
              reads=[b_dt1], writes=[b_ab2])
        mk.op("dve", lambda E: E.tensor_tensor(out=dt2, in0=dt2, in1=ab2[:, 0:NTQ], op=ALU.subtract),
              reads=[b_dt2, b_ab2], writes=[b_dt2])
        mk.op("act", lambda E: E.activation(out=dt2, in_=dt2, func=AF.Sqrt, bias=eps_sb),
              reads=[b_dt2, b_eps], writes=[b_dt2])
        mk.op("dve", lambda E: E.reciprocal(out=ab2[:, 0:NTQ], in_=dt2), reads=[b_dt2], writes=[b_ab2])
        mk.op("dve", lambda E: E.tensor_tensor(out=ab2[:, NTQ:2 * NTQ], in0=dt1,
                                               in1=ab2[:, 0:NTQ], op=ALU.mult),
              reads=[b_dt1, b_ab2], writes=[b_ab2])
        mk.op("dve", lambda E: E.tensor_scalar_mul(out=ab2[:, NTQ:2 * NTQ], in0=ab2[:, NTQ:2 * NTQ],
                                                   scalar1=-1.0),
              reads=[b_ab2], writes=[b_ab2])
        xn2, b_xn2v = A.alloc("xn2", NCD * NTQ, BF16)
        b_xn2 = A.subbufs(b_xn2v, NCD, "xn2_")
        for c in range(NCD):
            mk.op("dve", lambda E, c=c: E.tensor_tensor(out=dt1, in0=acc[:, c * NTQ:(c + 1) * NTQ],
                                                        in1=ab2[:, 0:NTQ], op=ALU.mult),
                  reads=[b_acc[c], b_ab2], writes=[b_dt1])
            mk.op("dve", lambda E, c=c: E.tensor_tensor(out=xn2[:, c * NTQ:(c + 1) * NTQ], in0=dt1,
                                                        in1=ab2[:, NTQ:2 * NTQ], op=ALU.add),
                  reads=[b_dt1, b_ab2], writes=[b_xn2[c]])
        A.free("ab2", "dt1", "dt2")

        if _trunc("D"):
            return nc
        gate_sb, b_gate = A.alloc("gatesb", NCD * 32, BF16, side="hi")
        gate_st, b_gatest = A.alloc("gatest", NCD * 32, F32, side="hi")
        mk.op("sp", lambda E: E.dma_start(out=gate_st, in_=gate_h), writes=[b_gatest], dma=True)
        mk.op("dve", lambda E: E.tensor_copy(out=gate_sb, in_=gate_st), reads=[b_gatest], writes=[b_gate])
        ident_sb, b_id = A.alloc("identsb", 128, F32, side="hi")
        mk.op("sp", lambda E: E.dma_start(out=ident_sb, in_=ident_h), writes=[b_id], dma=True)
        iota_sb, b_iota = A.alloc("iotasb", 32, F32, side="hi")
        mk.op("sp", lambda E: E.dma_start(out=iota_sb, in_=iota_h), writes=[b_iota], dma=True)
        sim_sb, b_sim = A.alloc("simsb", 32, F32, side="hi")
        mk.op("sp", lambda E: E.dma_start(out=sim_sb[0:32, :], in_=sim_h), writes=[b_sim], dma=True)
        sel_sb, b_sel = A.alloc("selsb", NE * 128, BF16)
        mk.op("sp", lambda E: E.dma_start(out=sel_sb[0:32, :], in_=sel_h), writes=[b_sel], dma=True)
        simd_sb, b_simd = A.alloc("simdsb", 32, F32, side="hi")
        mk.op("dve", lambda E: E.tensor_tensor(out=simd_sb[0:32, :], in0=sim_sb[0:32, :],
                                               in1=ident_sb[0:32, 0:32], op=ALU.mult),
              reads=[b_sim, b_id], writes=[b_simd])
        mk.op("dve", lambda E: E.tensor_tensor(out=simd_sb[0:32, :], in0=sim_sb[0:32, :],
                                               in1=simd_sb[0:32, :], op=ALU.subtract),
              reads=[b_sim, b_simd], writes=[b_simd])
        mk.op("dve", lambda E: E.tensor_tensor(out=simd_sb[0:32, :], in0=simd_sb[0:32, :],
                                               in1=ident_sb[0:32, 0:32], op=ALU.add),
              reads=[b_simd, b_id], writes=[b_simd])

        for c in range(NCD):
            for s2 in range(2):
                first = (c == 0)
                last = (c == NCD - 1 and s2 == 1)
                mk.op("pe", lambda E, c=c, s2=s2: E.matmul(
                        out=P[2][0:32, s2 * 512:(s2 + 1) * 512],
                        lhsT=gate_sb[:, c * 32:(c + 1) * 32],
                        rhs=xn2[:, c * NTQ + s2 * 512: c * NTQ + s2 * 512 + 512],
                        start=(c == 0), stop=(c == NCD - 1)),
                      reads=[b_gate, b_xn2[c]], writes=[bP[2]] if (first or last) else [], inc=last)
        Lsb, b_L = A.alloc("Lsb", 1024, F32, side="hi")
        mk.op("act", lambda E: E.copy(out=Lsb[0:32, :], in_=P[2][0:32, :]), reads=[bP[2]], writes=[b_L])
        Ltm, b_Ltm = A.alloc("Ltm", NQT * 32, F32, side="hi")
        for t in range(NQT):
            mk.op("pe", lambda E, t=t: E.transpose(P[3][0:128, 0:32], Lsb[0:32, t * 128:(t + 1) * 128],
                                                   ident_sb[0:32, 0:32]),
                  reads=[b_L, b_id], writes=[bP[3]])
            mk.op("act", lambda E, t=t: E.copy(out=Ltm[:, t * 32:(t + 1) * 32], in_=P[3][0:128, 0:32]),
                  reads=[bP[3]], writes=[b_Ltm])

        cT, b_cT = A.alloc("cT", 1024, F32, side="hi")
        cTb, b_cTb = A.alloc("cTb", 1024, BF16)
        r_ws, b_rws = A.alloc("rws", 64 * 8, F32, side="hi")
        rH, b_rH = A.alloc("rH", 8 * 32, F32, side="hi")
        rI, b_rI = A.alloc("rI", 8, U32, side="hi")
        for t in range(NQT):
            W0 = lambda c0, w: r_ws[:, c0:c0 + w]
            tile = Ltm[:, t * 32:(t + 1) * 32]
            mk.op("dve", lambda E, tile=tile: E.reduce_max(out=W0(0, 1), in_=tile, axis=AX.X, negate=True),
                  reads=[b_Ltm], writes=[b_rws])
            mk.op("act", lambda E, tile=tile: E.activation(out=W0(32, 32), in_=tile, func=AF.Exp,
                                                           bias=W0(0, 1), scale=1.0),
                  reads=[b_Ltm, b_rws], writes=[b_rws])
            mk.op("dve", lambda E: E.reduce_sum(out=W0(1, 1), in_=W0(32, 32), axis=AX.X),
                  reads=[b_rws], writes=[b_rws])
            mk.op("dve", lambda E: E.reciprocal(out=W0(2, 1), in_=W0(1, 1)),
                  reads=[b_rws], writes=[b_rws])
            mk.op("dve", lambda E: E.tensor_scalar(out=W0(32, 32), in0=W0(32, 32), scalar1=W0(2, 1),
                                                   scalar2=None, op0=ALU.mult),
                  reads=[b_rws], writes=[b_rws])
            mk.op("dve", lambda E: E.max(out=W0(8, 8), in_=W0(32, 32)), reads=[b_rws], writes=[b_rws])
            mk.op("dve", lambda E: E.max_index(out=rI, in_max=W0(8, 8), in_values=W0(32, 32)),
                  reads=[b_rws], writes=[b_rI])
            mk.op("dve", lambda E: E.tensor_copy(out=W0(16, 8), in_=rI), reads=[b_rI], writes=[b_rws])
            mk.op("dve", lambda E: E.reduce_sum(out=W0(3, 1), in_=W0(8, 8), axis=AX.X),
                  reads=[b_rws], writes=[b_rws])
            mk.op("dve", lambda E: E.tensor_scalar_max(out=W0(3, 1), in0=W0(3, 1), scalar1=1e-12),
                  reads=[b_rws], writes=[b_rws])
            mk.op("dve", lambda E: E.reciprocal(out=W0(4, 1), in_=W0(3, 1)), reads=[b_rws], writes=[b_rws])
            mk.op("dve", lambda E: E.tensor_scalar(out=W0(24, 8), in0=W0(8, 8), scalar1=W0(4, 1),
                                                   scalar2=None, op0=ALU.mult),
                  reads=[b_rws], writes=[b_rws])
            for k in range(8):
                mk.op("dve", lambda E, k=k: E.tensor_single_scalar(
                        out=rH[:, k * 32:(k + 1) * 32], in_=iota_sb[:, 0:32],
                        scalar=W0(16 + k, 1), op=ALU.is_equal),
                      reads=[b_rws, b_iota], writes=[b_rH])
            mk.op("dve", lambda E: E.tensor_scalar(out=W0(64, 32), in0=rH[:, 0:32], scalar1=W0(24, 1),
                                                   scalar2=None, op0=ALU.mult),
                  reads=[b_rH, b_rws], writes=[b_rws])
            for p2 in range(1, 4):
                mk.op("dve", lambda E, p2=p2: E.tensor_scalar(out=W0(96, 32), in0=rH[:, p2 * 32:(p2 + 1) * 32],
                                                              scalar1=W0(24 + p2, 1), scalar2=None,
                                                              op0=ALU.mult),
                      reads=[b_rH, b_rws], writes=[b_rws])
                mk.op("dve", lambda E: E.tensor_tensor(out=W0(64, 32), in0=W0(64, 32), in1=W0(96, 32),
                                                       op=ALU.add),
                      reads=[b_rws], writes=[b_rws])
            for k in range(4, 8):
                mk.op("pe", lambda E, k=k: E.transpose(P[3][0:32, 0:128], rH[:, k * 32:(k + 1) * 32],
                                                       ident_sb),
                      reads=[b_rH, b_id], writes=[bP[3]])
                mk.op("act", lambda E: E.copy(out=r_ws[0:32, 128:256], in_=P[3][0:32, 0:128]),
                      reads=[bP[3]], writes=[b_rws])
                mk.op("pe", lambda E: E.matmul(out=P[3][0:128, 0:32], lhsT=r_ws[0:32, 128:256],
                                               rhs=simd_sb[0:32, :], start=True, stop=True),
                      reads=[b_rws, b_simd], writes=[bP[3]])
                mk.op("act", lambda E: E.copy(out=W0(96, 32), in_=P[3][0:128, 0:32]),
                      reads=[bP[3]], writes=[b_rws])
                for p2 in range(4):
                    mk.op("dve", lambda E, p2=p2: E.tensor_tensor(out=W0(128, 32), in0=W0(96, 32),
                                                                  in1=rH[:, p2 * 32:(p2 + 1) * 32],
                                                                  op=ALU.mult),
                          reads=[b_rws, b_rH], writes=[b_rws])
                    mk.op("dve", lambda E, p2=p2: E.reduce_sum(out=W0(160 + p2, 1), in_=W0(128, 32),
                                                               axis=AX.X),
                          reads=[b_rws], writes=[b_rws])
                mk.op("dve", lambda E: E.reduce_max(out=W0(164, 1), in_=W0(160, 4), axis=AX.X),
                      reads=[b_rws], writes=[b_rws])
                mk.op("dve", lambda E: E.tensor_scalar(out=W0(165, 4), in0=W0(160, 4), scalar1=W0(164, 1),
                                                       scalar2=None, op0=ALU.is_equal),
                      reads=[b_rws], writes=[b_rws])
                mk.op("dve", lambda E: E.memset(W0(169, 1), 1.0), writes=[b_rws])
                for p2 in range(4):
                    mk.op("dve", lambda E, p2=p2: E.tensor_tensor(out=W0(170 + p2, 1), in0=W0(165 + p2, 1),
                                                                  in1=W0(169, 1), op=ALU.mult),
                          reads=[b_rws], writes=[b_rws])
                    if p2 < 3:
                        mk.op("dve", lambda E, p2=p2: E.tensor_scalar(out=W0(174, 1), in0=W0(165 + p2, 1),
                                                                      scalar1=-1.0, scalar2=1.0,
                                                                      op0=ALU.mult, op1=ALU.add),
                              reads=[b_rws], writes=[b_rws])
                        mk.op("dve", lambda E: E.tensor_tensor(out=W0(169, 1), in0=W0(169, 1),
                                                               in1=W0(174, 1), op=ALU.mult),
                              reads=[b_rws], writes=[b_rws])
                mk.op("dve", lambda E: E.tensor_scalar(out=W0(128, 32), in0=rH[:, 0:32], scalar1=W0(170, 1),
                                                       scalar2=None, op0=ALU.mult),
                      reads=[b_rH, b_rws], writes=[b_rws])
                for p2 in range(1, 4):
                    mk.op("dve", lambda E, p2=p2: E.tensor_scalar(out=W0(192, 32),
                                                                  in0=rH[:, p2 * 32:(p2 + 1) * 32],
                                                                  scalar1=W0(170 + p2, 1), scalar2=None,
                                                                  op0=ALU.mult),
                          reads=[b_rH, b_rws], writes=[b_rws])
                    mk.op("dve", lambda E: E.tensor_tensor(out=W0(128, 32), in0=W0(128, 32),
                                                           in1=W0(192, 32), op=ALU.add),
                          reads=[b_rws], writes=[b_rws])
                mk.op("dve", lambda E: E.tensor_scalar(out=W0(175, 1), in0=W0(164, 1), scalar1=0.5,
                                                       scalar2=None, op0=ALU.is_lt),
                      reads=[b_rws], writes=[b_rws])
                mk.op("dve", lambda E, k=k: E.tensor_scalar(out=W0(192, 32), in0=rH[:, k * 32:(k + 1) * 32],
                                                            scalar1=W0(175, 1), scalar2=None, op0=ALU.mult),
                      reads=[b_rH, b_rws], writes=[b_rws])
                mk.op("dve", lambda E: E.tensor_scalar(out=W0(176, 1), in0=W0(175, 1), scalar1=-1.0,
                                                       scalar2=1.0, op0=ALU.mult, op1=ALU.add),
                      reads=[b_rws], writes=[b_rws])
                mk.op("dve", lambda E: E.tensor_scalar(out=W0(128, 32), in0=W0(128, 32), scalar1=W0(176, 1),
                                                       scalar2=None, op0=ALU.mult),
                      reads=[b_rws], writes=[b_rws])
                mk.op("dve", lambda E: E.tensor_tensor(out=W0(128, 32), in0=W0(128, 32), in1=W0(192, 32),
                                                       op=ALU.add),
                      reads=[b_rws], writes=[b_rws])
                mk.op("dve", lambda E, k=k: E.tensor_scalar(out=W0(192, 32), in0=W0(128, 32),
                                                            scalar1=W0(24 + k, 1), scalar2=None,
                                                            op0=ALU.mult),
                      reads=[b_rws], writes=[b_rws])
                mk.op("dve", lambda E: E.tensor_tensor(out=W0(64, 32), in0=W0(64, 32), in1=W0(192, 32),
                                                       op=ALU.add),
                      reads=[b_rws], writes=[b_rws])
            mk.op("pe", lambda E: E.transpose(P[3][0:32, 0:128], W0(64, 32), ident_sb),
                  reads=[b_rws, b_id], writes=[bP[3]])
            mk.op("act", lambda E, t=t: E.copy(out=cT[0:32, t * 128:(t + 1) * 128], in_=P[3][0:32, 0:128]),
                  reads=[bP[3]], writes=[b_cT])
        mk.op("dve", lambda E: E.tensor_copy(out=cTb[0:32, :], in_=cT[0:32, :]), reads=[b_cT], writes=[b_cTb])
        A.free("gatesb", "gatest", "Lsb", "Ltm", "cT", "rws", "rH", "rI", "iotasb", "simsb", "simdsb")

        if _trunc("E"):
            return nc
        wdst, b_wdst = A.alloc("wdst", 1536, F32, nbufs=2, side="hi")
        wdbf, b_wdbf = A.alloc("wdbf", EG * NDJ * 128, BF16, nbufs=2, side="hi")
        h_sb, b_h = A.alloc("hsb", EG * NDJ * NTQ, BF16, side="hi")
        b_he = A.subbufs(b_h, EG, "h_")
        t1_sb, b_t1s = A.alloc("t1sb", NTQ, F32, nbufs=2, side="hi")
        t2_sb, b_t2s = A.alloc("t2sb", NTQ, F32, nbufs=2, side="hi")
        cbc_sb, b_cbc = A.alloc("cbcsb", NTQ, F32, nbufs=2, side="hi")
        JORD = [0, 6, 1, 7, 2, 8, 3, 9, 4, 10, 5, 11]
        for grp in range(NGRP):
            for ei in range(EG):
                e = grp * EG + ei
                cpar = e % 2
                pc = P[3 if (e % 2 == 0) else 2]
                bpc = bP[3 if (e % 2 == 0) else 2]
                for s2 in range(2):
                    mk.op("pe", lambda E, pc=pc, e=e, s2=s2: E.matmul(
                            out=pc[:, s2 * 512:(s2 + 1) * 512],
                            lhsT=sel_sb[0:32, e * 128:(e + 1) * 128],
                            rhs=cTb[0:32, s2 * 512:(s2 + 1) * 512],
                            start=True, stop=True),
                          reads=[b_sel, b_cTb], writes=[bpc] if s2 == 1 else [bpc], inc=(s2 == 1))
                mk.op("act", lambda E, pc=pc, cpar=cpar: E.copy(out=cbc_sb[cpar], in_=pc[:, :]),
                      reads=[bpc], writes=[b_cbc[cpar]])
                for ji, j in enumerate(JORD):
                    par = (ei * NJ + ji) % 2
                    mk.op("sp", lambda E, e=e, j=j, par=par: E.dma_start(out=wst[par], in_=wg_h[e, j]),
                          writes=[b_wst[par]], dma=True)
                    mk.op("dve", lambda E, par=par: E.tensor_copy(out=wbf[par], in_=wst[par]),
                          reads=[b_wst[par]], writes=[b_wbf[par]])
                    pp = 2 * ((ji // 2) % 2) + (0 if j < 6 else 1)
                    ps = P[pp]
                    bps = bP[pp]
                    for c in range(NCD):
                        for s2 in range(2):
                            lastmm = (c == NCD - 1 and s2 == 1)
                            firstmm = (c == 0)
                            mk.op("pe", lambda E, ps=ps, par=par, c=c, s2=s2: E.matmul(
                                    out=ps[:, s2 * 512:(s2 + 1) * 512],
                                    lhsT=wbf[par][:, c * 128:(c + 1) * 128],
                                    rhs=xn2[:, c * NTQ + s2 * 512: c * NTQ + s2 * 512 + 512],
                                    start=(c == 0), stop=(c == NCD - 1)),
                                  reads=[b_wbf[par], b_xn2[c]],
                                  writes=[bps] if (lastmm or firstmm) else [], inc=lastmm)
                    if j >= 6:
                        jj = j - 6
                        pg_ = P[2 * ((ji // 2) % 2)]
                        pu_ = P[2 * ((ji // 2) % 2) + 1]
                        bpg_ = bP[2 * ((ji // 2) % 2)]
                        bpu_ = bP[2 * ((ji // 2) % 2) + 1]
                        tpar = jj % 2
                        mk.op("act", lambda E, pg_=pg_, tpar=tpar: E.activation(
                                out=t1_sb[tpar], in_=pg_[:, :], func=AF.Silu),
                              reads=[bpg_], writes=[b_t1s[tpar]])
                        mk.op("dve", lambda E, pu_=pu_, tpar=tpar: E.tensor_tensor(
                                out=t2_sb[tpar], in0=t1_sb[tpar], in1=pu_[:, :], op=ALU.mult),
                              reads=[b_t1s[tpar], bpu_], writes=[b_t2s[tpar]])
                        mk.op("dve", lambda E, ei=ei, jj=jj, tpar=tpar, cpar=cpar: E.tensor_tensor(
                                out=h_sb[:, (ei * NDJ + jj) * NTQ:(ei * NDJ + jj + 1) * NTQ],
                                in0=t2_sb[tpar], in1=cbc_sb[cpar], op=ALU.mult),
                              reads=[b_t2s[tpar], b_cbc[cpar]], writes=[b_he[ei]])
            for m in range(NM):
                par = m % 2
                mk.op("sp", lambda E, grp=grp, m=m, par=par: E.dma_start(out=wdst[par], in_=wd_h[grp, m]),
                      writes=[b_wdst[par]], dma=True)
                mk.op("act", lambda E, par=par: E.copy(out=wdbf[par], in_=wdst[par]),
                      reads=[b_wdst[par]], writes=[b_wdbf[par]])
                ps = P[par]
                bps = bP[par]
                n_mm = EG * NDJ * 2
                i_mm = 0
                for ei in range(EG):
                    for j in range(NDJ):
                        for s2 in range(2):
                            i_mm += 1
                            lastmm = (i_mm == n_mm)
                            firstmm = (i_mm <= 2)
                            mk.op("pe", lambda E, ps=ps, par=par, ei=ei, j=j, s2=s2,
                                  first=(ei == 0 and j == 0): E.matmul(
                                    out=ps[:, s2 * 512:(s2 + 1) * 512],
                                    lhsT=wdbf[par][:, (ei * NDJ + j) * 128:(ei * NDJ + j + 1) * 128],
                                    rhs=h_sb[:, (ei * NDJ + j) * NTQ + s2 * 512:
                                             (ei * NDJ + j) * NTQ + s2 * 512 + 512],
                                    start=first, stop=(ei == EG - 1 and j == NDJ - 1)),
                                  reads=[b_wdbf[par], b_he[ei]],
                                  writes=[bps] if (lastmm or firstmm) else [], inc=lastmm)
                mk.op("dve", lambda E, ps=ps, m=m: E.tensor_tensor(
                        out=acc[:, m * NTQ:(m + 1) * NTQ], in0=acc[:, m * NTQ:(m + 1) * NTQ],
                        in1=ps[:, :], op=ALU.add),
                      reads=[bps, b_acc[m]], writes=[b_acc[m]])

        if _trunc("F"):
            return nc
        for c in range(NCD):
            mk.op("pool", lambda E, c=c: E.dma_start(out=out_h[c * 128:(c + 1) * 128, :],
                                                     in_=acc[:, c * NTQ:(c + 1) * NTQ]),
                  reads=[b_acc[c]], writes=[bOut[c]], dma=True)
        mk.wait_all("pool", bOut)

        mk.finish(st)
        n_inst = {k: len(v) for k, v in mk.prog.items()}
    return nc



def _f32(a):
    return np.ascontiguousarray(a, dtype=np.float32)


def _sbl(block):
    c = block.shape[0] // 128
    return np.transpose(block.reshape(c, 128, block.shape[1]), (1, 0, 2)).reshape(128, -1)


def _prep_shared(wq, wkvd, wku, wvu, wo, gate, wg, wd, simm):
    sh = {}
    sh["wq"] = np.ascontiguousarray(
        np.stack([_sbl(wq[:, m * 128:(m + 1) * 128]) for m in range(NH)]))
    sh["wkv"] = np.ascontiguousarray(
        np.stack([_sbl(wkvd[:, m * 128:(m + 1) * 128]) for m in range(NKV)]))
    sh["wku"] = np.ascontiguousarray(_sbl(wku))
    sh["wvu"] = np.ascontiguousarray(_sbl(wvu))
    sh["wo"] = np.ascontiguousarray(
        np.stack([_sbl(wo[:, m * 128:(m + 1) * 128]) for m in range(NM)]))
    sh["gate"] = np.ascontiguousarray(_sbl(gate.T))
    wgt = np.transpose(wg.reshape(NE, NJ, 128, NCD, 128), (0, 1, 3, 4, 2))
    sh["wg"] = np.ascontiguousarray(np.transpose(wgt, (0, 1, 3, 2, 4)).reshape(NE, NJ, 128, D))
    wdt = np.transpose(wd.reshape(NGRP, EG, NM, 128, NDJ, 128), (0, 2, 1, 4, 5, 3))
    sh["wd"] = np.ascontiguousarray(np.transpose(wdt, (0, 1, 4, 2, 3, 5)).reshape(NGRP, NM, 128, EG * NDJ * 128))
    sh["simm"] = np.ascontiguousarray(simm)
    sh["iota"] = np.broadcast_to(np.arange(32, dtype=np.float32), (128, 32)).copy()
    sh["ident"] = np.eye(128, dtype=np.float32)
    sel = np.zeros((32, NE * 128), dtype=ml_dtypes.bfloat16)
    for e in range(NE):
        sel[e, e * 128:(e + 1) * 128] = 1.0
    sh["sel"] = sel
    return sh


def _rope_tables(positions):
    inv = 10000.0 ** (-np.arange(0, HD, 2, dtype=np.float64) / HD)
    ang = positions[:, None].astype(np.float64) * inv[None, :]
    cos = np.concatenate([np.cos(ang), np.cos(ang)], axis=1).T
    sin = np.concatenate([np.sin(ang), np.sin(ang)], axis=1).T
    chi = cos.astype(np.float32)
    shi = sin.astype(np.float32)
    clo = (cos - chi).astype(ml_dtypes.bfloat16)
    slo = (sin - shi).astype(ml_dtypes.bfloat16)
    return (np.ascontiguousarray(np.stack([chi, shi])),
            np.ascontiguousarray(np.stack([clo, slo])))


def _per_core_consts(H):
    T0 = 8 * H
    q0 = 1024 * H
    pos = np.clip(np.arange(q0 - 512, q0 + 1024), 0, None)
    ropew, ropewl = _rope_tables(pos)
    ropeg, ropegl = _rope_tables(np.arange(0, T, 64))
    ki = np.arange(128)[:, None]
    qi = np.arange(128)[None, :]
    r4 = ((ki >= qi) | (ki % 64 == 0)).astype(np.float32)
    mc = np.zeros((128, MCTOT), dtype=ml_dtypes.bfloat16)
    for kk in range(NWC):
        gb = kk + T0 - 4
        lo = max(kk - 4, 0)
        hi = min(kk, 7)
        for lt in range(lo, hi + 1):
            colbase = MCOFF[kk] + (lt - lo) * 128
            r = kk - lt
            if gb < 0:
                blk = np.zeros((128, 128), np.float32)
            elif r == 0:
                blk = r4
            else:
                blk = np.ones((128, 128), np.float32)
            mc[:, colbase:colbase + 128] = blk.astype(ml_dtypes.bfloat16)
    mg = np.zeros((32, NQT), dtype=np.float32)
    for lt in range(NQT):
        ng = 2 * (T0 + lt - 4)
        if ng > 0:
            mg[0:min(ng, 32), lt] = 1.0
    return {"ropew": ropew, "ropewl": ropewl, "ropeg": ropeg, "ropegl": ropegl,
            "mc": mc, "maskg": np.ascontiguousarray(mg)}


def kernel(L0_attn_x, L0_attn_w_q, L0_attn_w_kv_down, L0_attn_w_k_up, L0_attn_w_v_up,
           L0_attn_w_o, L0_ffn_gate_weight, L0_ffn_gate_up_proj, L0_ffn_down_proj,
           L0_ffn_similarity):
    x = np.asarray(L0_attn_x)
    sh = _prep_shared(np.asarray(L0_attn_w_q), np.asarray(L0_attn_w_kv_down),
                      np.asarray(L0_attn_w_k_up), np.asarray(L0_attn_w_v_up),
                      np.asarray(L0_attn_w_o), np.asarray(L0_ffn_gate_weight),
                      np.asarray(L0_ffn_gate_up_proj), np.asarray(L0_ffn_down_proj),
                      np.asarray(L0_ffn_similarity))
    xt = np.ascontiguousarray(np.swapaxes(x, 1, 2))
    consts = [_per_core_consts(H) for H in range(2)]
    in_maps = []
    for core in range(8):
        b, H = core // 2, core % 2
        q0 = 1024 * H
        if H == 0:
            xwin = np.concatenate([np.zeros((D, 512), xt.dtype), xt[b][:, 0:1024]], axis=1)
        else:
            xwin = xt[b][:, 512:2048]
        m = {"x_win": np.ascontiguousarray(xwin),
             "x_glob": np.ascontiguousarray(_sbl(xt[b][:, ::64]))}
        m.update(sh)
        m.update(consts[H])
        in_maps.append(m)
    nc = build_nc()
    res = run_bass_kernel_spmd(nc, in_maps, core_ids=list(range(8)))
    out = np.empty((B, T, D), dtype=np.float32)
    for core in range(8):
        b, H = core // 2, core % 2
        out[b, 1024 * H:1024 * (H + 1), :] = np.asarray(res.results[core]["out"]).T
    return out
